# version 19 (final)
# baseline (speedup 1.0000x reference)
import contextlib

import numpy as np
import ml_dtypes
import concourse.bass as bass
import concourse.mybir as mybir
from concourse.bass_utils import run_bass_kernel_spmd

F32 = mybir.dt.float32
BF16 = mybir.dt.bfloat16
AF = mybir.ActivationFunctionType
ALU = mybir.AluOpType
AX = mybir.AxisListType

D = 4096
H = 32
SEQ = 2048
OWN = 1024
NT = 8
SCL = 1.0 / float(np.sqrt(128.0))
EPS = 1e-5
BF = ml_dtypes.bfloat16


class _Sm:
    __slots__ = ("h", "n")

    def __init__(self, h):
        self.h = h
        self.n = 0


def _build():
    nc = bass.Bass("TRN2", target_bir_lowering=False, debug=False)

    xo = nc.dram_tensor("x_own", [OWN, D], F32, kind="ExternalInput")
    xs = nc.dram_tensor("x_seq", [SEQ, D], F32, kind="ExternalInput")
    wqT = nc.dram_tensor("wqT", [32, 128, 32, 128], F32,
                         kind="ExternalInput")
    wkT = nc.dram_tensor("wkT", [32, 128, 32, 128], F32,
                         kind="ExternalInput")
    wvT = nc.dram_tensor("wvT", [32, 128, 32, 128], F32,
                         kind="ExternalInput")
    woT = nc.dram_tensor("woT", [32, 128, 32, 128], F32,
                         kind="ExternalInput")
    w12T = nc.dram_tensor("w12T", [64, 128, 32, 128], F32,
                          kind="ExternalInput")
    w3T = nc.dram_tensor("w3T", [32, 128, 32, 128], F32,
                         kind="ExternalInput")
    bqs = nc.dram_tensor("bqs", [128, 32], F32, kind="ExternalInput")
    bks = nc.dram_tensor("bks", [128, 32], F32, kind="ExternalInput")
    bvs = nc.dram_tensor("bvs", [128, 32], F32, kind="ExternalInput")
    bos = nc.dram_tensor("bos", [128, 32], F32, kind="ExternalInput")
    b3s = nc.dram_tensor("b3s", [128, 32], F32, kind="ExternalInput")
    b12s = nc.dram_tensor("b12s", [128, 64], F32, kind="ExternalInput")
    masks = nc.dram_tensor("masks", [16, 128, 512], BF16,
                           kind="ExternalInput")
    idn_d = nc.dram_tensor("idn", [128, 128], BF16, kind="ExternalInput")
    onec_d = nc.dram_tensor("onec", [128, 1], BF16, kind="ExternalInput")
    onem_d = nc.dram_tensor("onem", [128, 128], BF16, kind="ExternalInput")
    oner_d = nc.dram_tensor("oner", [1, 128], F32, kind="ExternalInput")
    onecf_d = nc.dram_tensor("onecf", [128, 1], F32, kind="ExternalInput")
    Kd = nc.dram_tensor("Kd", [H, 128, 16, 128], BF16)
    Vd = nc.dram_tensor("Vd", [H, 128, 16, 128], BF16)
    x1d = nc.dram_tensor("x1d", [NT, 128, D], F32)
    out = nc.dram_tensor("out", [OWN, D], F32, kind="ExternalOutput")

    es = contextlib.ExitStack()
    try:
        pA = es.enter_context(nc.sbuf_tensor("pA", [128, 32768], BF16))
        pB = es.enter_context(nc.sbuf_tensor("pB", [128, 32768], BF16))
        xb = es.enter_context(nc.sbuf_tensor("xb", [128, 8192], F32))
        wb = es.enter_context(nc.sbuf_tensor("wb", [128, 8192], BF16))
        kb = es.enter_context(nc.sbuf_tensor("kb", [128, 8192], BF16))
        ptb = es.enter_context(nc.sbuf_tensor("ptb", [128, 1536], BF16))
        fs = es.enter_context(nc.sbuf_tensor("fs", [128, 3072], F32))
        cbuf = es.enter_context(nc.sbuf_tensor("cbuf", [128, 336], BF16))

        p = [es.enter_context(nc.psum_tensor(f"ps{i}", [128, 512], F32))
             for i in range(6)]
        tp_es = contextlib.ExitStack()
        tp = [tp_es.enter_context(nc.psum_tensor(f"tps{i}", [128, 1024],
                                                 BF16)) for i in range(2)]

        names = ["sconst", "sx0", "sx1", "sw0", "sw1", "skv0", "skv1", "smk",
                 "skvw", "sx1w", "sst", "spe", "sac", "sdv"]
        S = {k: _Sm(es.enter_context(nc.semaphore(k))) for k in names}
        (sconst, sx0, sx1, sw0, sw1, skv0, skv1, smk, skvw, sx1w, sst, spe,
         sac, sdv) = (S[k] for k in names)
        sx = [sx0, sx1]
        sw = [sw0, sw1]
        skv = [skv0, skv1]

        BQ, BK, BV, BO, B3, B12 = 0, 32, 64, 96, 128, 160
        idn_ap = cbuf[:, 0:128]
        onec_ap = cbuf[:, 128:129]
        onem_ap = cbuf[:, 144:272]
        oner_ap = fs[0:1, 1920:2048]

        sched = []

        def _merge(waits):
            best = {}
            for it in waits:
                if it is None:
                    continue
                s, v = it
                if v is None or v <= 0:
                    continue
                k = id(s)
                if k not in best or best[k][1] < v:
                    best[k] = (s, v)
            return tuple((s.h, v) for s, v in best.values())

        def eop(en, make, inc=None, waits=()):
            v = None
            if inc is not None:
                inc.n += 1
                v = inc.n
            w = _merge(waits)

            def f(e, make=make, w=w, ih=(inc.h if inc else None)):
                for hh, val in w:
                    e.wait_ge(hh, val)
                i = make(e)
                if ih is not None:
                    i.then_inc(ih, 1)
            sched.append((en, f))
            return v

        def edma(en, s, make, waits=()):
            s.n += 16
            v = s.n
            w = _merge(waits)

            def f(e, make=make, w=w, h=s.h):
                for hh, val in w:
                    e.wait_ge(hh, val)
                make(e).then_inc(h, 16)
            sched.append((en, f))
            return v

        def ew(en, s, v):
            sched.append((en, lambda e, h=s.h, v=v: e.wait_ge(h, v)))

        for dst, src in [
            (cbuf[:, 0:128], idn_d.ap()), (cbuf[:, 128:129], onec_d.ap()),
            (cbuf[:, 144:272], onem_d.ap()),
            (fs[0:1, 1920:2048], oner_d.ap()),
            (fs[:, 960:961], onecf_d.ap()),
            (fs[:, BQ:BQ + 32], bqs.ap()), (fs[:, BK:BK + 32], bks.ap()),
            (fs[:, BV:BV + 32], bvs.ap()), (fs[:, BO:BO + 32], bos.ap()),
            (fs[:, B3:B3 + 32], b3s.ap()), (fs[:, B12:B12 + 64], b12s.ap()),
        ]:
            edma("sy", sconst, lambda e, d=dst, s=src: e.dma_start(out=d,
                                                                   in_=s))
        CT = sconst.n
        for en in ("te", "sc", "ve"):
            ew(en, sconst, CT)

        xfree = [[], []]
        wfree = [[], []]
        kbfree = [[], []]
        statfree = [None, None]
        tpfree = [[], []]
        pmark = [[] for _ in range(6)]
        tpctr = [0]
        gctr = [0]

        pA3 = pA.ap().rearrange("q (dc t) -> q dc t", dc=32)
        pB3 = pB.ap().rearrange("q (tt c) -> q tt c", tt=8)

        def ln_tile(xtile, pp, lnt, xwaits, lnwaits, sx_pre=None):
            cs = 224 + pp * 16
            sx_ = fs[:, cs + 0:cs + 1]
            sxx_ = fs[:, cs + 1:cs + 2]
            msq_ = fs[:, cs + 2:cs + 3]
            var_ = fs[:, cs + 3:cs + 4]
            sd_ = fs[:, cs + 4:cs + 5]
            r_ = fs[:, cs + 5:cs + 6]
            nmr_ = fs[:, cs + 6:cs + 7]
            exx_ = fs[:, cs + 7:cs + 8]
            sf = [statfree[pp]] if statfree[pp] else []
            if sx_pre is None:
                va = eop("ve", lambda e: e.reduce_sum(sx_, xtile, axis=AX.X),
                         inc=sdv, waits=list(xwaits) + sf)
                vaw = (sdv, va)
            else:
                vaw = sx_pre
            sb = eop("sc", lambda e: e.activation(lnt, xtile, AF.Square,
                                                  accum_out=sxx_),
                     inc=sac, waits=list(xwaits) + list(lnwaits) + sf)
            s1 = eop("sc", lambda e: e.activation(msq_, sx_, AF.Square,
                                                  scale=1.0 / D),
                     inc=sac, waits=[vaw] + sf)
            v2 = eop("ve", lambda e: e.tensor_scalar(
                exx_, sxx_, 1.0 / D, EPS, ALU.mult, ALU.add),
                inc=sdv, waits=[(sac, sb)])
            v3 = eop("ve", lambda e: e.tensor_sub(var_, exx_, msq_),
                     inc=sdv, waits=[(sac, s1), (sdv, v2)])
            s2 = eop("sc", lambda e: e.activation(sd_, var_, AF.Sqrt),
                     inc=sac, waits=[(sdv, v3)])
            v4 = eop("ve", lambda e: e.reciprocal(r_, sd_),
                     inc=sdv, waits=[(sac, s2)])
            v5 = eop("ve", lambda e: e.scalar_tensor_tensor(
                nmr_, sx_, -1.0 / D, r_, ALU.mult, ALU.mult),
                inc=sdv, waits=[(sdv, v4)])
            apm = eop("sc", lambda e: e.activation(lnt, xtile, AF.Identity,
                                                   scale=r_, bias=nmr_),
                      inc=sac, waits=[(sdv, v5)])
            statfree[pp] = (sac, apm)
            return apm

        def transpose_out(src_fn, n_blocks, out_fn, waits):
            last = None
            for g0 in range(0, n_blocks, 8):
                gw = min(8, n_blocks - g0)
                T = tpctr[0] % 2
                tpctr[0] += 1
                bank = tp[T]

                def mk(e, g0=g0, bank=bank, gw=gw):
                    for j in range(gw):
                        i = e.transpose(bank[:, j * 128:(j + 1) * 128],
                                        src_fn(g0 + j), idn_ap)
                    return i
                tm = eop("te", mk, inc=spe,
                         waits=list(waits) + list(tpfree[T]))
                ev = eop("ve",
                         lambda e, g0=g0, bank=bank, gw=gw: e.tensor_copy(
                             out_fn(g0, gw),
                             bank[:, 0:gw * 128].rearrange(
                                 "q (j c) -> q j c", j=gw)),
                         inc=sdv, waits=[(spe, tm)])
                tpfree[T] = [(sdv, ev)]
                last = ev
            return last

        def mm_group(bank, n, lhs_fn, rhs_fn, out_ap, waits):
            def mk(e, n=n):
                for i in range(n):
                    ins = e.matmul(out_ap, lhs_fn(i), rhs_fn(i),
                                   start=(i == 0), stop=(i == n - 1))
                return ins
            return eop("te", mk, inc=spe,
                       waits=list(waits) + list(pmark[bank]))

        def wslab(mat, col0, dstbuf, dstoff, s, waits):
            in_ = mat.ap()[col0 // 128].rearrange("q dc o -> q (dc o)")
            outa = dstbuf[:, dstoff:dstoff + 4096]
            return edma("gp", s,
                        lambda e, o=outa, i=in_: e.dma_start(out=o, in_=i),
                        waits=waits)

        lnt_of = [kb[:, 0:4096], kb[:, 4096:8192]]

        def emit_a_tile(tt, defer=False):
            pa = tt % 2
            xt = xb[:, pa * 4096:(pa + 1) * 4096]
            lv = edma("sy", sx[pa],
                      lambda e, t=tt, d=xt: e.dma_start(
                          out=d, in_=xo.ap()[t * 128:(t + 1) * 128, :]),
                      waits=xfree[pa])
            apm = ln_tile(xt, pa, lnt_of[pa], [(sx[pa], lv)], kbfree[pa])
            xfree[pa] = [(sac, apm)]

            def _tr(tt=tt, pa=pa, apm=apm):
                transpose_out(
                    lambda j, L=lnt_of[pa]: L[:, j * 128:(j + 1) * 128],
                    32,
                    lambda g0, gw, t=tt: pA3[:, g0:g0 + gw,
                                             t * 128:(t + 1) * 128],
                    [(sac, apm)])
                kbfree[pa] = [(spe, spe.n)]
            if defer:
                a_pend.append(_tr)
            else:
                _tr()

        for tt in range(4):
            emit_a_tile(tt)
        lnh = [(sdv, sdv.n), None]

        statfree45 = [[]]

        def proj_stats(ob, src_ap, maxcol, src_wait):
            sqz = kb[:, 2048:3072]
            sq = eop("sc", lambda e, s=src_ap: e.activation(sqz, s,
                                                            AF.Square),
                     inc=sac,
                     waits=list(statfree45[0]) + list(src_wait))
            m5 = eop("te", lambda e: (
                e.matmul(p[4][0:1, :], onec_ap, sqz[:, 0:512],
                         start=True, stop=True),
                e.matmul(p[5][0:1, :], onec_ap, sqz[:, 512:1024],
                         start=True, stop=True))[-1],
                inc=spe,
                waits=[(sac, sq)] + pmark[4] + pmark[5])
            t1 = fs[0:1, 352:353]
            t2 = fs[0:1, 353:354]
            cell = fs[0:1, maxcol + ob:maxcol + ob + 1]
            r1 = eop("ve", lambda e: (
                e.reduce_max(t1, p[4][0:1, :], axis=AX.X),
                e.reduce_max(t2, p[5][0:1, :], axis=AX.X))[-1],
                inc=sdv, waits=[(spe, m5)])
            r2 = eop("ve", lambda e: e.tensor_max(t1, t1, t2),
                     inc=sdv, waits=[(sdv, r1)])
            if maxcol == 256:
                rv = eop("ve", lambda e: e.tensor_copy(cell, t1),
                         inc=sdv, waits=[(sdv, r2)])
            else:
                rv = eop("ve", lambda e: e.tensor_max(cell, cell, t1),
                         inc=sdv, waits=[(sdv, r2)])
            pmark[4] = [(sdv, rv)]
            pmark[5] = [(sdv, rv)]
            statfree45[0] = [(spe, m5)]
            return rv

        km_init = eop("ve", lambda e: e.memset(fs[0:1, 288:320], -1.0),
                      inc=sdv)
        pend_st = [None]

        def flush_stats():
            if pend_st[0] is not None:
                pend_st[0]()
                pend_st[0] = None

        qready = 0
        pend_tiles = [4, 5, 6, 7]
        a_pend = []
        b2_pre = {"n": 0, "marks": []}

        def emit_b2h0_tile(tt):
            pa = tt % 2
            xt = xb[:, pa * 4096:(pa + 1) * 4096]
            lv = edma("sy", sx[pa],
                      lambda e, t=tt, d=xt: e.dma_start(
                          out=d, in_=xs.ap()[t * 128:(t + 1) * 128, :]),
                      waits=xfree[pa])
            apm = ln_tile(xt, pa, lnt_of[pa], [(sx[pa], lv)], lnfree0[pa])
            xfree[pa] = [(sac, apm)]
            transpose_out(
                lambda j, L=lnt_of[pa]: L[:, j * 128:(j + 1) * 128], 32,
                lambda g0, gw, t=tt: pA3[:, g0:g0 + gw,
                                         t * 128:(t + 1) * 128],
                [(sac, apm), b1_tq0_pe[0]])
            b2_pre["marks"].append((spe, spe.n))

        b1_tq0_pe = [(spe, 0)]
        lnfree0 = None
        for tq in range(2):
            if tq == 1:
                while a_pend:
                    a_pend.pop(0)()
                if lnh[1] is None:
                    lnh[1] = (sdv, sdv.n)
                b1_tq0_pe[0] = (spe, spe.n)
            for ob in range(H):
                if tq == 0 and ob % 8 == 0 and pend_tiles:
                    emit_a_tile(pend_tiles.pop(0), defer=True)
                if tq == 0 and ob % 8 == 4 and a_pend:
                    a_pend.pop(0)()
                    if not pend_tiles and not a_pend:
                        lnh[1] = (sdv, sdv.n)
                if tq == 1 and ob % 8 == 4 and b2_pre["n"] < 4:
                    if lnfree0 is None:
                        lnfree0 = [list(kbfree[0]), list(kbfree[1])]
                    emit_b2h0_tile(b2_pre["n"])
                    b2_pre["n"] += 1
                pw = ob % 2
                wv = wslab(wqT, ob * 128, wb, pw * 4096, sw[pw], wfree[pw])
                g = gctr[0] % 4
                gctr[0] += 1
                gm = mm_group(
                    g, 32,
                    lambda i, pw=pw: wb[:, pw * 4096 + i * 128:
                                        pw * 4096 + (i + 1) * 128],
                    lambda i, tq=tq: pA[:, i * 1024 + tq * 512:
                                        i * 1024 + tq * 512 + 512],
                    p[g][:, :],
                    [(sw[pw], wv), lnh[tq]])
                ev = eop("sc", lambda e, g=g, ob=ob, tq=tq: e.activation(
                    pB[:, ob * 1024 + tq * 512:ob * 1024 + tq * 512 + 512],
                    p[g][:, :], AF.Identity,
                    bias=fs[:, BQ + ob:BQ + ob + 1]),
                    inc=sac, waits=[(spe, gm)])
                pmark[g] = [(sac, ev)]
                qready = ev
                wfree[pw] = [(spe, gm)]
                if tq == 1:
                    flush_stats()
                    pend_st[0] = (lambda ob=ob, qr=qready: proj_stats(
                        ob, pB[:, ob * 1024:(ob + 1) * 1024], 256,
                        kbfree[0] + kbfree[1] + [(sac, qr)]))
        flush_stats()
        b1_end_pe = (spe, spe.n)

        pa_free_pe = [b1_end_pe]
        lnfree = [list(kbfree[0]) + statfree45[0],
                  list(kbfree[1])]
        for h2 in range(2):
            lntr = [[], []]
            if h2 == 0:
                lntr[0] = list(b2_pre["marks"])
                lntr[1] = list(b2_pre["marks"])
                lnfree[0] = lnfree[0] + list(b2_pre["marks"])
                lnfree[1] = lnfree[1] + list(b2_pre["marks"])
            for tt in range(4 if h2 == 0 else 0, NT):
                pa = tt % 2
                xt = xb[:, pa * 4096:(pa + 1) * 4096]
                lv = edma("sy", sx[pa],
                          lambda e, t=tt, h2=h2, d=xt: e.dma_start(
                              out=d,
                              in_=xs.ap()[h2 * 1024 + t * 128:
                                          h2 * 1024 + (t + 1) * 128, :]),
                          waits=xfree[pa])
                apm = ln_tile(xt, pa, lnt_of[pa], [(sx[pa], lv)],
                              lnfree[pa])
                xfree[pa] = [(sac, apm)]
                transpose_out(
                    lambda j, L=lnt_of[pa]: L[:, j * 128:(j + 1) * 128], 32,
                    lambda g0, gw, t=tt: pA3[:, g0:g0 + gw,
                                             t * 128:(t + 1) * 128],
                    [(sac, apm)] + pa_free_pe)
                lntr[pa] = [(spe, spe.n)]
            ln_dv = (sdv, sdv.n)

            kstaR = [[], []]
            for ob in range(H):
                pw = ob % 2
                pk = ob % 2
                wv = wslab(wkT, ob * 128, wb, pw * 4096, sw[pw], wfree[pw])
                ksta = kb[:, pk * 1024:(pk + 1) * 1024]
                lastev = None
                gm = None
                for tq in range(2):
                    g = gctr[0] % 4
                    gctr[0] += 1
                    gm = mm_group(
                        g, 32,
                        lambda i, pw=pw: wb[:, pw * 4096 + i * 128:
                                            pw * 4096 + (i + 1) * 128],
                        lambda i, tq=tq: pA[:, i * 1024 + tq * 512:
                                            i * 1024 + tq * 512 + 512],
                        p[g][:, :],
                        [(sw[pw], wv), ln_dv])
                    lastev = eop(
                        "sc", lambda e, g=g, K=ksta, tq=tq, ob=ob:
                        e.activation(
                            K[:, tq * 512:(tq + 1) * 512], p[g][:, :],
                            AF.Identity, bias=fs[:, BK + ob:BK + ob + 1]),
                        inc=sac,
                        waits=[(spe, gm)] + kstaR[pk] + lntr[0])
                    pmark[g] = [(sac, lastev)]
                wfree[pw] = [(spe, gm)]
                kwr = edma("sy", skvw,
                           lambda e, h2=h2, ob=ob, K=ksta: e.dma_start(
                               out=Kd.ap()[ob, :, h2 * 8:(h2 + 1) * 8, :],
                               in_=K.rearrange("q (tb k) -> q tb k", tb=8)),
                           waits=[(sac, lastev)])
                flush_stats()

                def _st(ob=ob, ksta=ksta, lastev=lastev, kwr=kwr, pk=pk):
                    sv = proj_stats(ob, ksta, 288,
                                    [(sac, lastev)] + lntr[0])
                    kstaR[pk] = [(skvw, kwr), (sdv, sv)]
                pend_st[0] = _st
            flush_stats()
            vstaR = [[], []]
            vsta2R = [[], []]
            pend_v = [None]

            def flush_v():
                if pend_v[0] is not None:
                    pend_v[0]()
                    pend_v[0] = None
            for ob in range(H):
                pw = ob % 2
                pv = ob % 2
                wv = wslab(wvT, ob * 128, wb, pw * 4096, sw[pw], wfree[pw])
                vsta = kb[:, 4096 + pv * 1024:4096 + (pv + 1) * 1024]
                vsta2 = kb[:, 6144 + pv * 1024:6144 + (pv + 1) * 1024]
                lastev = None
                gm = None
                for tq in range(2):
                    g = gctr[0] % 4
                    gctr[0] += 1
                    gm = mm_group(
                        g, 32,
                        lambda i, pw=pw: wb[:, pw * 4096 + i * 128:
                                            pw * 4096 + (i + 1) * 128],
                        lambda i, tq=tq: pA[:, i * 1024 + tq * 512:
                                            i * 1024 + tq * 512 + 512],
                        p[g][:, :],
                        [(sw[pw], wv), ln_dv])
                    lastev = eop(
                        "sc", lambda e, g=g, V=vsta, tq=tq, ob=ob:
                        e.activation(
                            V[:, tq * 512:(tq + 1) * 512], p[g][:, :],
                            AF.Identity, bias=fs[:, BV + ob:BV + ob + 1]),
                        inc=sac, waits=[(spe, gm)] + vstaR[pv] + lntr[1])
                    pmark[g] = [(sac, lastev)]
                wfree[pw] = [(spe, gm)]
                flush_v()

                def _vt(ob=ob, pv=pv, vsta=vsta, vsta2=vsta2,
                        lastev=lastev, h2=h2):
                    trm = transpose_out(
                        lambda j, V=vsta: V[:, j * 128:(j + 1) * 128], 8,
                        lambda g0, gw, V2=vsta2: V2[:, 0:1024].rearrange(
                            "q (j c) -> q j c", j=8),
                        [(sac, lastev)] + vsta2R[pv] + lntr[1])
                    vstaR[pv] = [(spe, spe.n)]
                    vwr = edma("sy", skvw,
                               lambda e, h2=h2, ob=ob, V2=vsta2:
                               e.dma_start(
                                   out=Vd.ap()[ob, :,
                                               h2 * 8:(h2 + 1) * 8, :],
                                   in_=V2.rearrange(
                                       "q (tb hd) -> q tb hd", tb=8)),
                               waits=[(sdv, trm)])
                    vsta2R[pv] = [(skvw, vwr)]
                pend_v[0] = _vt
            flush_v()
            pa_free_pe = [(spe, spe.n)]
            lnfree = [kstaR[0] + kstaR[1] + statfree45[0],
                      vstaR[0] + vstaR[1] + vsta2R[0] + vsta2R[1]]
        kvw_total = skvw.n
        b2_end_pe = (spe, spe.n)
        kbfree = [list(lnfree[0]), list(lnfree[1])]

        qstat_dv = (sdv, sdv.n)
        nb_row = fs[0:1, 896:928]
        v1 = eop("ve", lambda e: e.tensor_mul(
            nb_row, fs[0:1, 256:288], fs[0:1, 288:320]), inc=sdv,
            waits=[qstat_dv])
        s1 = eop("sc", lambda e: e.activation(nb_row, nb_row, AF.Sqrt),
                 inc=sac, waits=[(sdv, v1)])
        v2 = eop("ve", lambda e: e.tensor_scalar_mul(nb_row, nb_row, -SCL),
                 inc=sdv, waits=[(sac, s1)])
        mB = eop("te", lambda e: e.matmul(
            p[4][:, 0:32], oner_ap, nb_row, start=True, stop=True),
            inc=spe, waits=[(sdv, v2)] + pmark[4])
        negb_mark = eop("ve", lambda e: e.tensor_copy(
            fs[:, 320:352], p[4][:, 0:32]), inc=sdv, waits=[(spe, mB)])
        pmark[4] = [(sdv, negb_mark)]
        pmark[4] = [(sdv, negb_mark)]

        for i in range(16):
            dst = (wb[:, i * 512:(i + 1) * 512] if i < 8 else
                   wb[:, 4096 + (i - 8) * 512:4096 + (i - 7) * 512])
            edma("sy", smk,
                 lambda e, i=i, d=dst: e.dma_start(out=d,
                                                   in_=masks.ap()[i]),
                 waits=wfree[0] + wfree[1])
        MK = smk.n

        tp_marks_ab = list(tpfree[0]) + list(tpfree[1])
        tp_es.close()
        s2_es = contextlib.ExitStack()
        s2 = s2_es.enter_context(nc.psum_tensor("s2c", [128, 512], F32))
        S = [p[0], p[1], s2]
        sb_expmark = [pmark[0], pmark[1], list(tp_marks_ab)]
        ptfree_m = [[], [], []]

        avfree = [pmark[2], pmark[3]]
        sumfree = [pmark[4], pmark[5]]
        kvfree_c = [list(kbfree[0]), list(kbfree[1])]
        mask_end = [[], []]
        for pos in range(2):
            n_c = 8 if pos == 0 else 16
            for h in range(H):
                ph = h % 2
                koff = ph * 4096
                voff = ph * 4096 + 2048
                edma("sy", skv[ph],
                     lambda e, h=h, koff=koff, n_c=n_c: e.dma_start(
                         out=kb[:, koff:koff + n_c * 128],
                         in_=Kd.ap()[h, :, 0:n_c, :].rearrange(
                             "q tb k -> q (tb k)")),
                     waits=[(skvw, kvw_total)] + kvfree_c[ph])
                edma("sy", skv[ph],
                     lambda e, h=h, voff=voff, n_c=n_c: e.dma_start(
                         out=kb[:, voff:voff + n_c * 128],
                         in_=Vd.ap()[h, :, 0:n_c, :].rearrange(
                             "q tb k -> q (tb k)")),
                     waits=[(skvw, kvw_total)] + kvfree_c[ph])
                kv_v = skv[ph].n
                qrhs = pB[:, h * 1024 + pos * 512:h * 1024 + pos * 512 + 512]

                def emit_score(c):
                    sb = c % 3
                    scm = eop("te", lambda e, sb=sb, c=c, koff=koff,
                              q=qrhs: e.matmul(
                                  S[sb][:, :],
                                  kb[:, koff + c * 128:koff + (c + 1) * 128],
                                  q, start=True, stop=True),
                              inc=spe,
                              waits=[(skv[ph], kv_v), (sac, qready)] +
                              sb_expmark[sb])
                    exm = eop("sc", lambda e, sb=sb, h=h: e.activation(
                        ptb[:, sb * 512:(sb + 1) * 512], S[sb][:, :],
                        AF.Exp, bias=fs[:, 320 + h:321 + h], scale=SCL),
                        inc=sac,
                        waits=[(spe, scm), (sdv, negb_mark)] + ptfree_m[sb])
                    sb_expmark[sb] = [(sac, exm)]
                    masked = (pos == 0) or (c >= 8)
                    if masked:
                        moff = c * 512 if pos == 0 else 4096 + (c - 8) * 512
                        mk = eop("ve", lambda e, sb=sb, moff=moff:
                                 e.tensor_mul(
                                     ptb[:, sb * 512:(sb + 1) * 512],
                                     ptb[:, sb * 512:(sb + 1) * 512],
                                     wb[:, moff:moff + 512]),
                                 inc=sdv, waits=[(sac, exm), (smk, MK)])
                        mask_end[0 if pos == 0 else 1] = [(sdv, mk)]
                        return [(sdv, mk)]
                    return [(sac, exm)]

                def emit_avsums(c, av_w):
                    sb = c % 3
                    first = (c == 0)
                    lastc = (c == n_c - 1)
                    sm_ = eop(
                        "te", lambda e, sb=sb, c=c, ph=ph, voff=voff,
                        first=first, lastc=lastc: (
                            e.matmul(p[2 + ph][:, :],
                                     kb[:, voff + c * 128:
                                        voff + (c + 1) * 128],
                                     ptb[:, sb * 512:(sb + 1) * 512],
                                     start=first, stop=lastc),
                            e.matmul(p[4 + ph][:, :], onem_ap,
                                     ptb[:, sb * 512:(sb + 1) * 512],
                                     start=first, stop=lastc))[-1],
                        inc=spe,
                        waits=av_w + ((avfree[ph] + sumfree[ph])
                                      if first else []))
                    ptfree_m[sb] = [(spe, sm_)]
                    return sm_

                av_w_q = []
                last_sums = None
                for c in range(n_c):
                    av_w_q.append(emit_score(c))
                    if c >= 2:
                        last_sums = emit_avsums(c - 2, av_w_q[c - 2])
                last_sums = emit_avsums(n_c - 2, av_w_q[n_c - 2])
                last_sums = emit_avsums(n_c - 1, av_w_q[n_c - 1])
                rbz = fs[:, 384:896] if ph == 0 else fs[:, 2048:2560]
                rcp = eop("ve", lambda e, ph=ph, z=rbz: e.reciprocal(
                    z, p[4 + ph][:, :]), inc=sdv,
                    waits=[(spe, last_sums)])
                oev = eop("ve", lambda e, ph=ph, h=h, pos=pos, z=rbz:
                          e.tensor_mul(
                              pA[:, h * 1024 + pos * 512:
                                 h * 1024 + pos * 512 + 512],
                              p[2 + ph][:, :], z),
                          inc=sdv,
                          waits=[(spe, last_sums), b2_end_pe, (sdv, rcp)])
                sumfree[ph] = [(sdv, rcp)]
                avfree[ph] = [(sdv, oev)]
                kvfree_c[ph] = [(spe, last_sums)]
        c_end_dv = (sdv, sdv.n)
        c_end_pe = (spe, spe.n)
        pmark[0] = sb_expmark[0]
        pmark[1] = sb_expmark[1]
        pmark[2] = avfree[0]
        pmark[3] = avfree[1]
        pmark[4] = sumfree[0]
        pmark[5] = sumfree[1]
        kbfree = [kvfree_c[0], kvfree_c[1]]
        afree = [list(ptfree_m[0]), list(ptfree_m[1])]
        s2_es.close()
        tp_alias_marks = (list(sb_expmark[2]) + list(tp_marks_ab))
        tp[0] = es.enter_context(nc.psum_tensor("tpd0", [128, 1024], BF16))
        tp[1] = es.enter_context(nc.psum_tensor("tpd1", [128, 1024], BF16))
        tpfree[0] = list(tp_alias_marks)
        tpfree[1] = list(tp_alias_marks)

        pend = [None]

        def flush_pend():
            if pend[0] is not None:
                pend[0]()
                pend[0] = None

        def tmproj_phase(mat, bias_base, rhs_dv_wait, pb_pe_wait,
                         slab_extra):
            for ob in range(H):
                pw = ob % 2
                wv = wslab(mat, ob * 128, wb, pw * 4096, sw[pw],
                           wfree[pw] + slab_extra[pw])
                gm = None
                for tq in range(2):
                    g = gctr[0] % 4
                    gctr[0] += 1
                    gm = mm_group(
                        g, 32,
                        lambda i, pw=pw: wb[:, pw * 4096 + i * 128:
                                            pw * 4096 + (i + 1) * 128],
                        lambda i, tq=tq: pA[:, i * 1024 + tq * 512:
                                            i * 1024 + tq * 512 + 512],
                        p[g][:, :],
                        [(sw[pw], wv), rhs_dv_wait])
                    flush_pend()
                    jp = (2 * ob + tq) % 2
                    af = ptb[:, jp * 512:(jp + 1) * 512]
                    ev = eop("sc", lambda e, g=g, af=af, ob=ob: e.activation(
                        af, p[g][:, :], AF.Identity,
                        bias=fs[:, bias_base + ob:bias_base + ob + 1]),
                        inc=sac, waits=[(spe, gm)] + afree[jp])
                    pmark[g] = [(sac, ev)]

                    def emit_tr(ev=ev, af=af, ob=ob, tq=tq, jp=jp):
                        T = tpctr[0] % 2
                        tpctr[0] += 1
                        bank = tp[T]

                        def mk(e, bank=bank, af=af):
                            for j in range(4):
                                i = e.transpose(
                                    bank[:, j * 128:(j + 1) * 128],
                                    af[:, j * 128:(j + 1) * 128], idn_ap)
                            return i
                        tm = eop("te", mk, inc=spe,
                                 waits=[(sac, ev)] + tpfree[T])
                        afree[jp] = [(spe, tm)]
                        evv = eop("ve",
                                  lambda e, bank=bank, ob=ob, tq=tq:
                                  e.tensor_copy(
                                      pB3[:, tq * 4:(tq + 1) * 4,
                                          ob * 128:(ob + 1) * 128],
                                      bank[:, 0:512].rearrange(
                                          "q (j c) -> q j c", j=4)),
                                  inc=sdv,
                                  waits=[(spe, tm), pb_pe_wait])
                        tpfree[T] = [(sdv, evv)]
                    pend[0] = emit_tr
                wfree[pw] = [(spe, gm)]
            flush_pend()

        tmproj_phase(woT, BO, c_end_dv, c_end_pe,
                     [mask_end[0], mask_end[1]])
        d1_end_dv = (sdv, sdv.n)
        d1_end_pe = (spe, spe.n)

        for tt in range(NT):
            pa = tt % 2
            xt = xb[:, pa * 4096:(pa + 1) * 4096]
            lv = edma("sy", sx[pa],
                      lambda e, t=tt, d=xt: e.dma_start(
                          out=d, in_=xo.ap()[t * 128:(t + 1) * 128, :]),
                      waits=xfree[pa] + [d1_end_dv])
            cs0 = 224 + pa * 16
            sf0 = [statfree[pa]] if statfree[pa] else []
            ad = eop("ve", lambda e, t=tt, x=xt, c=cs0:
                     e.scalar_tensor_tensor(
                         x, pB[:, t * 4096:(t + 1) * 4096], 1.0, x,
                         ALU.mult, ALU.add,
                         accum_out=fs[:, c:c + 1]),
                inc=sdv, waits=[(sx[pa], lv), d1_end_dv] + sf0)
            sv = edma("sy", sx1w,
                      lambda e, t=tt, x=xt: e.dma_start(
                          out=x1d.ap()[t], in_=x),
                      waits=[(sdv, ad)])
            apm = ln_tile(xt, pa, lnt_of[pa], [(sdv, ad)], kbfree[pa],
                          sx_pre=(sdv, ad))
            xfree[pa] = [(sac, apm), (sx1w, sv)]
            transpose_out(
                lambda j, L=lnt_of[pa]: L[:, j * 128:(j + 1) * 128], 32,
                lambda g0, gw, t=tt: pB[:, t * 4096 + g0 * 128:
                                        t * 4096 + (g0 + gw) * 128]
                .rearrange("q (j c) -> q j c", j=gw),
                [(sac, apm), (sdv, ad)])
            kbfree[pa] = [(spe, spe.n)]
        d1b_dv = (sdv, sdv.n)

        d2free = [wfree[0] + wfree[1] + mask_end[1],
                  kbfree[0] + kbfree[1]]
        ssfree = [[], []]
        d2lastg = [None, None]
        for mc in range(H):
            bsel = mc % 2
            buf = wb if bsel == 0 else kb
            ssem = sw[0] if bsel == 0 else skv[0]
            wl = d2free[bsel] if d2lastg[bsel] is None else \
                [(spe, d2lastg[bsel])]
            w1 = wslab(w12T, mc * 128, buf, 0, ssem, wl)
            w2 = wslab(w12T, (mc + 32) * 128, buf, 4096, ssem, wl)
            for tq in range(2):
                pm = (2 * mc + tq) % 2
                ssz = fs[:, 1408:1920] if pm == 0 else fs[:, 2560:3072]
                gA = gctr[0] % 4
                gctr[0] += 1
                gB = gctr[0] % 4
                gctr[0] += 1
                gmA = mm_group(
                    gA, 32,
                    lambda i, buf=buf: buf[:, i * 128:(i + 1) * 128],
                    lambda i, tq=tq: pB3[:, tq * 4:(tq + 1) * 4,
                                         i * 128:(i + 1) * 128],
                    p[gA][:, :],
                    [(ssem, w2), d1b_dv])
                gmB = mm_group(
                    gB, 32,
                    lambda i, buf=buf: buf[:, 4096 + i * 128:
                                           4096 + (i + 1) * 128],
                    lambda i, tq=tq: pB3[:, tq * 4:(tq + 1) * 4,
                                         i * 128:(i + 1) * 128],
                    p[gB][:, :],
                    [(ssem, w2), d1b_dv])
                sil = eop("sc", lambda e, gA=gA, mc=mc, z=ssz: e.activation(
                    z, p[gA][:, :], AF.Silu,
                    bias=fs[:, B12 + mc:B12 + mc + 1]),
                    inc=sac, waits=[(spe, gmA)] + ssfree[pm])
                x4m = eop("ve", lambda e, gB=gB, mc=mc, tq=tq, z=ssz:
                          e.scalar_tensor_tensor(
                              pA[:, mc * 1024 + tq * 512:
                                 mc * 1024 + tq * 512 + 512],
                              p[gB][:, :],
                              fs[:, B12 + 32 + mc:B12 + 32 + mc + 1],
                              z, ALU.add, ALU.mult),
                          inc=sdv,
                          waits=[(spe, gmB), (sac, sil), d1_end_pe])
                pmark[gA] = [(sac, sil)]
                pmark[gB] = [(sdv, x4m)]
                ssfree[pm] = [(sdv, x4m)]
                d2lastg[bsel] = gmB
        d2_end_dv = (sdv, sdv.n)
        d2_end_pe = (spe, spe.n)

        wfree[0] = [(spe, d2lastg[0])]
        wfree[1] = [(spe, d2lastg[0])]
        tmproj_phase(w3T, B3, d2_end_dv, d2_end_pe, [[], []])
        d3_end_dv = (sdv, sdv.n)

        for tt in range(NT):
            pa = tt % 2
            xt = xb[:, pa * 4096:(pa + 1) * 4096]
            lv = edma("sy", sx[pa],
                      lambda e, t=tt, d=xt: e.dma_start(
                          out=d, in_=x1d.ap()[t]),
                      waits=xfree[pa] + [(sx1w, sx1w.n), d3_end_dv])
            ad = eop("ve", lambda e, t=tt, x=xt: e.tensor_add(
                x, pB[:, t * 4096:(t + 1) * 4096], x),
                inc=sdv, waits=[(sx[pa], lv), d3_end_dv])
            edma("sy", sst,
                 lambda e, t=tt, x=xt: e.dma_start(
                     out=out.ap()[t * 128:(t + 1) * 128, :], in_=x),
                 waits=[(sdv, ad)])
            xfree[pa] = [(sst, sst.n)]
        ew("sy", sst, sst.n)

        with nc.Block() as block:
            @block.sync
            def _(e):
                for en, f in sched:
                    if en == "sy":
                        f(e)

            @block.gpsimd
            def _(e):
                for en, f in sched:
                    if en == "gp":
                        f(e)

            @block.tensor
            def _(e):
                for en, f in sched:
                    if en == "te":
                        f(e)

            @block.scalar
            def _(e):
                for en, f in sched:
                    if en == "sc":
                        f(e)

            @block.vector
            def _(e):
                for en, f in sched:
                    if en == "ve":
                        f(e)
    finally:
        es.close()
    return nc


def _host_masks(par):
    qg = (0, 3) if par == 0 else (1, 2)
    m = np.zeros((16, 128, 512), np.float32)
    q = np.arange(512)[None, :]
    kk = np.arange(128)[:, None]
    for c in range(8):
        m[c] = ((c * 128 + kk) <= (qg[0] * 512 + q))
    for j in range(8):
        c = 8 + j
        m[c] = ((c * 128 + kk) <= (qg[1] * 512 + q))
    return np.ascontiguousarray(m.astype(BF))


def kernel(L0_attn_x, L0_attn_Wq, L0_attn_bq, L0_attn_Wk, L0_attn_bk,
           L0_attn_Wv, L0_attn_bv, L0_attn_Wo, L0_attn_bo,
           L0_ffn_w12, L0_ffn_b12, L0_ffn_w3, L0_ffn_b3):
    x = np.asarray(L0_attn_x, np.float32)

    def T(w):
        wt = np.asarray(w, np.float32).T
        nb = wt.shape[1] // 128
        return np.ascontiguousarray(
            wt.reshape(32, 128, nb, 128).transpose(2, 1, 0, 3))

    def shape_bias(b, n):
        return np.ascontiguousarray(
            np.asarray(b, np.float32).reshape(n, 128).T)

    wq, wk, wv, wo = (T(w) for w in
                      (L0_attn_Wq, L0_attn_Wk, L0_attn_Wv, L0_attn_Wo))
    w12, w3 = T(L0_ffn_w12), T(L0_ffn_w3)
    bq, bk, bv, bo, b3 = (shape_bias(b, 32) for b in
                          (L0_attn_bq, L0_attn_bk, L0_attn_bv, L0_attn_bo,
                           L0_ffn_b3))
    b12 = shape_bias(L0_ffn_b12, 64)
    idn = np.eye(128, dtype=BF)
    onec = np.ones((128, 1), dtype=BF)
    onem = np.ones((128, 128), dtype=BF)
    onecf = np.ones((128, 1), dtype=np.float32)
    oner = np.ones((1, 128), dtype=np.float32)
    msk = [_host_masks(0), _host_masks(1)]

    nc = _build()
    in_maps = []
    for c in range(8):
        seq, par = c // 2, c % 2
        xseq = np.ascontiguousarray(x[seq])
        if par == 0:
            xown = np.ascontiguousarray(
                np.concatenate([x[seq, 0:512], x[seq, 1536:2048]], axis=0))
        else:
            xown = np.ascontiguousarray(x[seq, 512:1536])
        in_maps.append(dict(
            x_own=xown, x_seq=xseq, wqT=wq, wkT=wk, wvT=wv, woT=wo,
            w12T=w12, w3T=w3, bqs=bq, bks=bk, bvs=bv, bos=bo, b3s=b3,
            b12s=b12, masks=msk[par], idn=idn, onec=onec, onecf=onecf,
            oner=oner, onem=onem))
    res = run_bass_kernel_spmd(nc, in_maps, core_ids=list(range(8)))
    y = np.empty((4, 2048, 4096), np.float32)
    for c in range(8):
        seq, par = c // 2, c % 2
        o = np.asarray(res.results[c]["out"])
        if par == 0:
            y[seq, 0:512] = o[0:512]
            y[seq, 1536:2048] = o[512:1024]
        else:
            y[seq, 512:1536] = o
    return y


# version 20
# speedup vs baseline: 1.0476x; 1.0476x over previous
import contextlib

import numpy as np
import ml_dtypes
import concourse.bass as bass
import concourse.mybir as mybir
from concourse.bass_utils import run_bass_kernel_spmd

F32 = mybir.dt.float32
BF16 = mybir.dt.bfloat16
AF = mybir.ActivationFunctionType
ALU = mybir.AluOpType
AX = mybir.AxisListType

D = 4096
H = 32
SEQ = 2048
OWN = 1024
NT = 8
SCL = 1.0 / float(np.sqrt(128.0))
EPS = 1e-5
BF = ml_dtypes.bfloat16


class _Sm:
    __slots__ = ("h", "n")

    def __init__(self, h):
        self.h = h
        self.n = 0


def _build():
    nc = bass.Bass("TRN2", target_bir_lowering=False, debug=False)

    xo = nc.dram_tensor("x_own", [OWN, D], F32, kind="ExternalInput")
    xs = nc.dram_tensor("x_seq", [SEQ, D], F32, kind="ExternalInput")
    wqT = nc.dram_tensor("wqT", [32, 128, 32, 128], F32,
                         kind="ExternalInput")
    wkT = nc.dram_tensor("wkT", [32, 128, 32, 128], F32,
                         kind="ExternalInput")
    wvT = nc.dram_tensor("wvT", [32, 128, 32, 128], F32,
                         kind="ExternalInput")
    woT = nc.dram_tensor("woT", [32, 128, 32, 128], F32,
                         kind="ExternalInput")
    w12T = nc.dram_tensor("w12T", [64, 128, 32, 128], F32,
                          kind="ExternalInput")
    w3T = nc.dram_tensor("w3T", [32, 128, 32, 128], F32,
                         kind="ExternalInput")
    bqs = nc.dram_tensor("bqs", [128, 32], F32, kind="ExternalInput")
    bks = nc.dram_tensor("bks", [128, 32], F32, kind="ExternalInput")
    bvs = nc.dram_tensor("bvs", [128, 32], F32, kind="ExternalInput")
    bos = nc.dram_tensor("bos", [128, 32], F32, kind="ExternalInput")
    b3s = nc.dram_tensor("b3s", [128, 32], F32, kind="ExternalInput")
    b12s = nc.dram_tensor("b12s", [128, 64], F32, kind="ExternalInput")
    masks = nc.dram_tensor("masks", [16, 128, 512], BF16,
                           kind="ExternalInput")
    idn_d = nc.dram_tensor("idn", [128, 128], BF16, kind="ExternalInput")
    onec_d = nc.dram_tensor("onec", [128, 1], BF16, kind="ExternalInput")
    onem_d = nc.dram_tensor("onem", [128, 128], BF16, kind="ExternalInput")
    oner_d = nc.dram_tensor("oner", [1, 128], F32, kind="ExternalInput")
    onecf_d = nc.dram_tensor("onecf", [128, 1], F32, kind="ExternalInput")
    Kd = nc.dram_tensor("Kd", [H, 128, 16, 128], BF16)
    Vd = nc.dram_tensor("Vd", [H, 128, 16, 128], BF16)
    x1d = nc.dram_tensor("x1d", [NT, 128, D], F32)
    out = nc.dram_tensor("out", [OWN, D], F32, kind="ExternalOutput")

    es = contextlib.ExitStack()
    try:
        pA = es.enter_context(nc.sbuf_tensor("pA", [128, 32768], BF16))
        pB = es.enter_context(nc.sbuf_tensor("pB", [128, 32768], BF16))
        xb = es.enter_context(nc.sbuf_tensor("xb", [128, 8192], F32))
        wb = es.enter_context(nc.sbuf_tensor("wb", [128, 8192], BF16))
        kb = es.enter_context(nc.sbuf_tensor("kb", [128, 8192], BF16))
        ptb = es.enter_context(nc.sbuf_tensor("ptb", [128, 1536], BF16))
        fs = es.enter_context(nc.sbuf_tensor("fs", [128, 3072], F32))
        cbuf = es.enter_context(nc.sbuf_tensor("cbuf", [128, 336], BF16))

        p = [es.enter_context(nc.psum_tensor(f"ps{i}", [128, 512], F32))
             for i in range(6)]
        tp_es = contextlib.ExitStack()
        tp = [tp_es.enter_context(nc.psum_tensor(f"tps{i}", [128, 1024],
                                                 BF16)) for i in range(2)]

        names = ["sconst", "sx0", "sx1", "sw0", "sw1", "skv0", "skv1", "smk",
                 "skvw", "sx1w", "sst", "spe", "sac", "sdv"]
        S = {k: _Sm(es.enter_context(nc.semaphore(k))) for k in names}
        (sconst, sx0, sx1, sw0, sw1, skv0, skv1, smk, skvw, sx1w, sst, spe,
         sac, sdv) = (S[k] for k in names)
        sx = [sx0, sx1]
        sw = [sw0, sw1]
        skv = [skv0, skv1]

        BQ, BK, BV, BO, B3, B12 = 0, 32, 64, 96, 128, 160
        idn_ap = cbuf[:, 0:128]
        onec_ap = cbuf[:, 128:129]
        onem_ap = cbuf[:, 144:272]
        oner_ap = fs[0:1, 1920:2048]

        sched = []

        def _merge(waits):
            best = {}
            for it in waits:
                if it is None:
                    continue
                s, v = it
                if v is None or v <= 0:
                    continue
                k = id(s)
                if k not in best or best[k][1] < v:
                    best[k] = (s, v)
            return tuple((s.h, v) for s, v in best.values())

        def eop(en, make, inc=None, waits=()):
            v = None
            if inc is not None:
                inc.n += 1
                v = inc.n
            w = _merge(waits)

            def f(e, make=make, w=w, ih=(inc.h if inc else None)):
                for hh, val in w:
                    e.wait_ge(hh, val)
                i = make(e)
                if ih is not None:
                    i.then_inc(ih, 1)
            sched.append((en, f))
            return v

        def edma(en, s, make, waits=()):
            s.n += 16
            v = s.n
            w = _merge(waits)

            def f(e, make=make, w=w, h=s.h):
                for hh, val in w:
                    e.wait_ge(hh, val)
                make(e).then_inc(h, 16)
            sched.append((en, f))
            return v

        def ew(en, s, v):
            sched.append((en, lambda e, h=s.h, v=v: e.wait_ge(h, v)))

        for dst, src in [
            (cbuf[:, 0:128], idn_d.ap()), (cbuf[:, 128:129], onec_d.ap()),
            (cbuf[:, 144:272], onem_d.ap()),
            (fs[0:1, 1920:2048], oner_d.ap()),
            (fs[:, 960:961], onecf_d.ap()),
            (fs[:, BQ:BQ + 32], bqs.ap()), (fs[:, BK:BK + 32], bks.ap()),
            (fs[:, BV:BV + 32], bvs.ap()), (fs[:, BO:BO + 32], bos.ap()),
            (fs[:, B3:B3 + 32], b3s.ap()), (fs[:, B12:B12 + 64], b12s.ap()),
        ]:
            edma("sy", sconst, lambda e, d=dst, s=src: e.dma_start(out=d,
                                                                   in_=s))
        CT = sconst.n
        for en in ("te", "sc", "ve"):
            ew(en, sconst, CT)

        xfree = [[], []]
        wfree = [[], []]
        kbfree = [[], []]
        statfree = [None, None]
        tpfree = [[], []]
        pmark = [[] for _ in range(6)]
        tpctr = [0]
        gctr = [0]

        pA3 = pA.ap().rearrange("q (dc t) -> q dc t", dc=32)
        pB3 = pB.ap().rearrange("q (tt c) -> q tt c", tt=8)

        def ln_tile(xtile, pp, lnt, xwaits, lnwaits, sx_pre=None):
            cs = 224 + pp * 16
            sx_ = fs[:, cs + 0:cs + 1]
            sxx_ = fs[:, cs + 1:cs + 2]
            msq_ = fs[:, cs + 2:cs + 3]
            var_ = fs[:, cs + 3:cs + 4]
            sd_ = fs[:, cs + 4:cs + 5]
            r_ = fs[:, cs + 5:cs + 6]
            nmr_ = fs[:, cs + 6:cs + 7]
            exx_ = fs[:, cs + 7:cs + 8]
            sf = [statfree[pp]] if statfree[pp] else []
            if sx_pre is None:
                va = eop("ve", lambda e: e.reduce_sum(sx_, xtile, axis=AX.X),
                         inc=sdv, waits=list(xwaits) + sf)
                vaw = (sdv, va)
            else:
                vaw = sx_pre
            sb = eop("sc", lambda e: e.activation(lnt, xtile, AF.Square,
                                                  accum_out=sxx_),
                     inc=sac, waits=list(xwaits) + list(lnwaits) + sf)
            s1 = eop("sc", lambda e: e.activation(msq_, sx_, AF.Square,
                                                  scale=1.0 / D),
                     inc=sac, waits=[vaw] + sf)
            v2 = eop("ve", lambda e: e.tensor_scalar(
                exx_, sxx_, 1.0 / D, EPS, ALU.mult, ALU.add),
                inc=sdv, waits=[(sac, sb)])
            v3 = eop("ve", lambda e: e.tensor_sub(var_, exx_, msq_),
                     inc=sdv, waits=[(sac, s1), (sdv, v2)])
            s2 = eop("sc", lambda e: e.activation(sd_, var_, AF.Sqrt),
                     inc=sac, waits=[(sdv, v3)])
            v4 = eop("ve", lambda e: e.reciprocal(r_, sd_),
                     inc=sdv, waits=[(sac, s2)])
            v5 = eop("ve", lambda e: e.scalar_tensor_tensor(
                nmr_, sx_, -1.0 / D, r_, ALU.mult, ALU.mult),
                inc=sdv, waits=[(sdv, v4)])
            apm = eop("sc", lambda e: e.activation(lnt, xtile, AF.Identity,
                                                   scale=r_, bias=nmr_),
                      inc=sac, waits=[(sdv, v5)])
            statfree[pp] = (sac, apm)
            return apm

        def transpose_out(src_fn, n_blocks, out_fn, waits):
            last = None
            for g0 in range(0, n_blocks, 8):
                gw = min(8, n_blocks - g0)
                T = tpctr[0] % 2
                tpctr[0] += 1
                bank = tp[T]

                def mk(e, g0=g0, bank=bank, gw=gw):
                    for j in range(gw):
                        i = e.transpose(bank[:, j * 128:(j + 1) * 128],
                                        src_fn(g0 + j), idn_ap)
                    return i
                tm = eop("te", mk, inc=spe,
                         waits=list(waits) + list(tpfree[T]))
                ev = eop("ve",
                         lambda e, g0=g0, bank=bank, gw=gw: e.tensor_copy(
                             out_fn(g0, gw),
                             bank[:, 0:gw * 128].rearrange(
                                 "q (j c) -> q j c", j=gw)),
                         inc=sdv, waits=[(spe, tm)])
                tpfree[T] = [(sdv, ev)]
                last = ev
            return last

        def mm_group(bank, n, lhs_fn, rhs_fn, out_ap, waits):
            def mk(e, n=n):
                for i in range(n):
                    ins = e.matmul(out_ap, lhs_fn(i), rhs_fn(i),
                                   start=(i == 0), stop=(i == n - 1))
                return ins
            return eop("te", mk, inc=spe,
                       waits=list(waits) + list(pmark[bank]))

        def wslab(mat, col0, dstbuf, dstoff, s, waits):
            in_ = mat.ap()[col0 // 128].rearrange("q dc o -> q (dc o)")
            outa = dstbuf[:, dstoff:dstoff + 4096]
            return edma("gp", s,
                        lambda e, o=outa, i=in_: e.dma_start(out=o, in_=i),
                        waits=waits)

        lnt_of = [kb[:, 0:4096], kb[:, 4096:8192]]

        def emit_a_tile(tt, defer=False):
            pa = tt % 2
            xt = xb[:, pa * 4096:(pa + 1) * 4096]
            lv = edma("sy", sx[pa],
                      lambda e, t=tt, d=xt: e.dma_start(
                          out=d, in_=xo.ap()[t * 128:(t + 1) * 128, :]),
                      waits=xfree[pa])
            apm = ln_tile(xt, pa, lnt_of[pa], [(sx[pa], lv)], kbfree[pa])
            xfree[pa] = [(sac, apm)]

            def _tr(tt=tt, pa=pa, apm=apm):
                transpose_out(
                    lambda j, L=lnt_of[pa]: L[:, j * 128:(j + 1) * 128],
                    32,
                    lambda g0, gw, t=tt: pA3[:, g0:g0 + gw,
                                             t * 128:(t + 1) * 128],
                    [(sac, apm)])
                kbfree[pa] = [(spe, spe.n)]
            if defer:
                a_pend.append(_tr)
            else:
                _tr()

        for tt in range(4):
            emit_a_tile(tt)
        lnh = [(sdv, sdv.n), None]

        statfree45 = [[]]

        def proj_stats(ob, src_ap, maxcol, src_wait):
            sqz = kb[:, 2048:3072]
            sq = eop("sc", lambda e, s=src_ap: e.activation(sqz, s,
                                                            AF.Square),
                     inc=sac,
                     waits=list(statfree45[0]) + list(src_wait))
            m5 = eop("te", lambda e: (
                e.matmul(p[4][0:1, :], onec_ap, sqz[:, 0:512],
                         start=True, stop=True),
                e.matmul(p[5][0:1, :], onec_ap, sqz[:, 512:1024],
                         start=True, stop=True))[-1],
                inc=spe,
                waits=[(sac, sq)] + pmark[4] + pmark[5])
            t1 = fs[0:1, 352:353]
            t2 = fs[0:1, 353:354]
            cell = fs[0:1, maxcol + ob:maxcol + ob + 1]
            r1 = eop("ve", lambda e: (
                e.reduce_max(t1, p[4][0:1, :], axis=AX.X),
                e.reduce_max(t2, p[5][0:1, :], axis=AX.X))[-1],
                inc=sdv, waits=[(spe, m5)])
            r2 = eop("ve", lambda e: e.tensor_max(t1, t1, t2),
                     inc=sdv, waits=[(sdv, r1)])
            if maxcol == 256:
                rv = eop("ve", lambda e: e.tensor_copy(cell, t1),
                         inc=sdv, waits=[(sdv, r2)])
            else:
                rv = eop("ve", lambda e: e.tensor_max(cell, cell, t1),
                         inc=sdv, waits=[(sdv, r2)])
            pmark[4] = [(sdv, rv)]
            pmark[5] = [(sdv, rv)]
            statfree45[0] = [(spe, m5)]
            return rv

        km_init = eop("ve", lambda e: e.memset(fs[0:1, 288:320], -1.0),
                      inc=sdv)
        pend_st = [None]

        def flush_stats():
            if pend_st[0] is not None:
                pend_st[0]()
                pend_st[0] = None

        qready = 0
        pend_tiles = [4, 5, 6, 7]
        a_pend = []
        b2_pre = {"n": 0, "marks": []}

        def emit_b2h0_tile(tt):
            pa = tt % 2
            xt = xb[:, pa * 4096:(pa + 1) * 4096]
            lv = edma("sy", sx[pa],
                      lambda e, t=tt, d=xt: e.dma_start(
                          out=d, in_=xs.ap()[t * 128:(t + 1) * 128, :]),
                      waits=xfree[pa])
            apm = ln_tile(xt, pa, lnt_of[pa], [(sx[pa], lv)], lnfree0[pa])
            xfree[pa] = [(sac, apm)]
            transpose_out(
                lambda j, L=lnt_of[pa]: L[:, j * 128:(j + 1) * 128], 32,
                lambda g0, gw, t=tt: pA3[:, g0:g0 + gw,
                                         t * 128:(t + 1) * 128],
                [(sac, apm), b1_tq0_pe[0]])
            b2_pre["marks"].append((spe, spe.n))

        b1_tq0_pe = [(spe, 0)]
        lnfree0 = None
        for tq in range(2):
            if tq == 1:
                while a_pend:
                    a_pend.pop(0)()
                if lnh[1] is None:
                    lnh[1] = (sdv, sdv.n)
                b1_tq0_pe[0] = (spe, spe.n)
            for ob in range(H):
                if tq == 0 and ob % 8 == 0 and pend_tiles:
                    emit_a_tile(pend_tiles.pop(0), defer=True)
                if tq == 0 and ob % 8 == 4 and a_pend:
                    a_pend.pop(0)()
                    if not pend_tiles and not a_pend:
                        lnh[1] = (sdv, sdv.n)
                if tq == 1 and ob % 8 == 4 and b2_pre["n"] < 4:
                    if lnfree0 is None:
                        lnfree0 = [list(kbfree[0]), list(kbfree[1])]
                    emit_b2h0_tile(b2_pre["n"])
                    b2_pre["n"] += 1
                pw = ob % 2
                wv = wslab(wqT, ob * 128, wb, pw * 4096, sw[pw], wfree[pw])
                g = gctr[0] % 4
                gctr[0] += 1
                gm = mm_group(
                    g, 32,
                    lambda i, pw=pw: wb[:, pw * 4096 + i * 128:
                                        pw * 4096 + (i + 1) * 128],
                    lambda i, tq=tq: pA[:, i * 1024 + tq * 512:
                                        i * 1024 + tq * 512 + 512],
                    p[g][:, :],
                    [(sw[pw], wv), lnh[tq]])
                ev = eop("sc", lambda e, g=g, ob=ob, tq=tq: e.activation(
                    pB[:, ob * 1024 + tq * 512:ob * 1024 + tq * 512 + 512],
                    p[g][:, :], AF.Identity,
                    bias=fs[:, BQ + ob:BQ + ob + 1]),
                    inc=sac, waits=[(spe, gm)])
                pmark[g] = [(sac, ev)]
                qready = ev
                wfree[pw] = [(spe, gm)]
                if tq == 1:
                    flush_stats()
                    pend_st[0] = (lambda ob=ob, qr=qready: proj_stats(
                        ob, pB[:, ob * 1024:(ob + 1) * 1024], 256,
                        kbfree[0] + kbfree[1] + [(sac, qr)]))
        flush_stats()
        b1_end_pe = (spe, spe.n)

        pa_free_pe = [b1_end_pe]
        lnfree = [list(kbfree[0]) + statfree45[0],
                  list(kbfree[1])]
        for h2 in range(2):
            lntr = [[], []]
            if h2 == 0:
                lntr[0] = list(b2_pre["marks"])
                lntr[1] = list(b2_pre["marks"])
                lnfree[0] = lnfree[0] + list(b2_pre["marks"])
                lnfree[1] = lnfree[1] + list(b2_pre["marks"])
            for tt in range(4 if h2 == 0 else 0, NT):
                pa = tt % 2
                xt = xb[:, pa * 4096:(pa + 1) * 4096]
                lv = edma("sy", sx[pa],
                          lambda e, t=tt, h2=h2, d=xt: e.dma_start(
                              out=d,
                              in_=xs.ap()[h2 * 1024 + t * 128:
                                          h2 * 1024 + (t + 1) * 128, :]),
                          waits=xfree[pa])
                apm = ln_tile(xt, pa, lnt_of[pa], [(sx[pa], lv)],
                              lnfree[pa])
                xfree[pa] = [(sac, apm)]
                transpose_out(
                    lambda j, L=lnt_of[pa]: L[:, j * 128:(j + 1) * 128], 32,
                    lambda g0, gw, t=tt: pA3[:, g0:g0 + gw,
                                             t * 128:(t + 1) * 128],
                    [(sac, apm)] + pa_free_pe)
                lntr[pa] = [(spe, spe.n)]
            ln_dv = (sdv, sdv.n)

            kstaR = [[], []]
            for ob in range(H):
                pw = ob % 2
                pk = ob % 2
                wv = wslab(wkT, ob * 128, wb, pw * 4096, sw[pw], wfree[pw])
                ksta = kb[:, pk * 1024:(pk + 1) * 1024]
                lastev = None
                gm = None
                for tq in range(2):
                    g = gctr[0] % 4
                    gctr[0] += 1
                    gm = mm_group(
                        g, 32,
                        lambda i, pw=pw: wb[:, pw * 4096 + i * 128:
                                            pw * 4096 + (i + 1) * 128],
                        lambda i, tq=tq: pA[:, i * 1024 + tq * 512:
                                            i * 1024 + tq * 512 + 512],
                        p[g][:, :],
                        [(sw[pw], wv), ln_dv])
                    lastev = eop(
                        "sc", lambda e, g=g, K=ksta, tq=tq, ob=ob:
                        e.activation(
                            K[:, tq * 512:(tq + 1) * 512], p[g][:, :],
                            AF.Identity, bias=fs[:, BK + ob:BK + ob + 1]),
                        inc=sac,
                        waits=[(spe, gm)] + kstaR[pk] + lntr[0])
                    pmark[g] = [(sac, lastev)]
                wfree[pw] = [(spe, gm)]
                kwr = edma("sy", skvw,
                           lambda e, h2=h2, ob=ob, K=ksta: e.dma_start(
                               out=Kd.ap()[ob, :, h2 * 8:(h2 + 1) * 8, :],
                               in_=K.rearrange("q (tb k) -> q tb k", tb=8)),
                           waits=[(sac, lastev)])
                flush_stats()

                def _st(ob=ob, ksta=ksta, lastev=lastev, kwr=kwr, pk=pk):
                    sv = proj_stats(ob, ksta, 288,
                                    [(sac, lastev)] + lntr[0])
                    kstaR[pk] = [(skvw, kwr), (sdv, sv)]
                pend_st[0] = _st
            flush_stats()
            vstaR = [[], []]
            vsta2R = [[], []]
            pend_v = [None]

            def flush_v():
                if pend_v[0] is not None:
                    pend_v[0]()
                    pend_v[0] = None
            for ob in range(H):
                pw = ob % 2
                pv = ob % 2
                wv = wslab(wvT, ob * 128, wb, pw * 4096, sw[pw], wfree[pw])
                vsta = kb[:, 4096 + pv * 1024:4096 + (pv + 1) * 1024]
                vsta2 = kb[:, 6144 + pv * 1024:6144 + (pv + 1) * 1024]
                lastev = None
                gm = None
                for tq in range(2):
                    g = gctr[0] % 4
                    gctr[0] += 1
                    gm = mm_group(
                        g, 32,
                        lambda i, pw=pw: wb[:, pw * 4096 + i * 128:
                                            pw * 4096 + (i + 1) * 128],
                        lambda i, tq=tq: pA[:, i * 1024 + tq * 512:
                                            i * 1024 + tq * 512 + 512],
                        p[g][:, :],
                        [(sw[pw], wv), ln_dv])
                    lastev = eop(
                        "sc", lambda e, g=g, V=vsta, tq=tq, ob=ob:
                        e.activation(
                            V[:, tq * 512:(tq + 1) * 512], p[g][:, :],
                            AF.Identity, bias=fs[:, BV + ob:BV + ob + 1]),
                        inc=sac, waits=[(spe, gm)] + vstaR[pv] + lntr[1])
                    pmark[g] = [(sac, lastev)]
                wfree[pw] = [(spe, gm)]
                flush_v()

                def _vt(ob=ob, pv=pv, vsta=vsta, vsta2=vsta2,
                        lastev=lastev, h2=h2):
                    trm = transpose_out(
                        lambda j, V=vsta: V[:, j * 128:(j + 1) * 128], 8,
                        lambda g0, gw, V2=vsta2: V2[:, 0:1024].rearrange(
                            "q (j c) -> q j c", j=8),
                        [(sac, lastev)] + vsta2R[pv] + lntr[1])
                    vstaR[pv] = [(spe, spe.n)]
                    vwr = edma("sy", skvw,
                               lambda e, h2=h2, ob=ob, V2=vsta2:
                               e.dma_start(
                                   out=Vd.ap()[ob, :,
                                               h2 * 8:(h2 + 1) * 8, :],
                                   in_=V2.rearrange(
                                       "q (tb hd) -> q tb hd", tb=8)),
                               waits=[(sdv, trm)])
                    vsta2R[pv] = [(skvw, vwr)]
                pend_v[0] = _vt
            flush_v()
            pa_free_pe = [(spe, spe.n)]
            lnfree = [kstaR[0] + kstaR[1] + statfree45[0],
                      vstaR[0] + vstaR[1] + vsta2R[0] + vsta2R[1]]
        kvw_total = skvw.n
        b2_end_pe = (spe, spe.n)
        kbfree = [list(lnfree[0]), list(lnfree[1])]

        qstat_dv = (sdv, sdv.n)
        nb_row = fs[0:1, 896:928]
        v1 = eop("ve", lambda e: e.tensor_mul(
            nb_row, fs[0:1, 256:288], fs[0:1, 288:320]), inc=sdv,
            waits=[qstat_dv])
        s1 = eop("sc", lambda e: e.activation(nb_row, nb_row, AF.Sqrt),
                 inc=sac, waits=[(sdv, v1)])
        v2 = eop("ve", lambda e: e.tensor_scalar_mul(nb_row, nb_row, -SCL),
                 inc=sdv, waits=[(sac, s1)])
        mB = eop("te", lambda e: e.matmul(
            p[4][:, 0:32], oner_ap, nb_row, start=True, stop=True),
            inc=spe, waits=[(sdv, v2)] + pmark[4])
        negb_mark = eop("ve", lambda e: e.tensor_copy(
            fs[:, 320:352], p[4][:, 0:32]), inc=sdv, waits=[(spe, mB)])
        pmark[4] = [(sdv, negb_mark)]
        pmark[4] = [(sdv, negb_mark)]

        for i in range(16):
            dst = (wb[:, i * 512:(i + 1) * 512] if i < 8 else
                   wb[:, 4096 + (i - 8) * 512:4096 + (i - 7) * 512])
            edma("sy", smk,
                 lambda e, i=i, d=dst: e.dma_start(out=d,
                                                   in_=masks.ap()[i]),
                 waits=wfree[0] + wfree[1])
        MK = smk.n

        tp_marks_ab = list(tpfree[0]) + list(tpfree[1])
        tp_es.close()
        s2_es = contextlib.ExitStack()
        s2 = s2_es.enter_context(nc.psum_tensor("s2c", [128, 512], F32))
        S = [p[0], p[1], s2]
        sb_expmark = [pmark[0], pmark[1], list(tp_marks_ab)]
        ptfree_m = [[], [], []]

        avfree = [pmark[2], pmark[3]]
        sumfree = [pmark[4], pmark[5]]
        kvfree_c = [list(kbfree[0]), list(kbfree[1])]
        mask_end = [[], []]
        for pos in range(2):
            n_c = 8 if pos == 0 else 16
            for h in range(H):
                ph = h % 2
                koff = ph * 4096
                voff = ph * 4096 + 2048
                edma("sy", skv[ph],
                     lambda e, h=h, koff=koff, n_c=n_c: e.dma_start(
                         out=kb[:, koff:koff + n_c * 128],
                         in_=Kd.ap()[h, :, 0:n_c, :].rearrange(
                             "q tb k -> q (tb k)")),
                     waits=[(skvw, kvw_total)] + kvfree_c[ph])
                edma("gp", skv[ph],
                     lambda e, h=h, voff=voff, n_c=n_c: e.dma_start(
                         out=kb[:, voff:voff + n_c * 128],
                         in_=Vd.ap()[h, :, 0:n_c, :].rearrange(
                             "q tb k -> q (tb k)")),
                     waits=[(skvw, kvw_total)] + kvfree_c[ph])
                kv_v = skv[ph].n
                qrhs = pB[:, h * 1024 + pos * 512:h * 1024 + pos * 512 + 512]

                def emit_score(c):
                    sb = c % 3
                    scm = eop("te", lambda e, sb=sb, c=c, koff=koff,
                              q=qrhs: e.matmul(
                                  S[sb][:, :],
                                  kb[:, koff + c * 128:koff + (c + 1) * 128],
                                  q, start=True, stop=True),
                              inc=spe,
                              waits=[(skv[ph], kv_v), (sac, qready)] +
                              sb_expmark[sb])
                    exm = eop("sc", lambda e, sb=sb, h=h: e.activation(
                        ptb[:, sb * 512:(sb + 1) * 512], S[sb][:, :],
                        AF.Exp, bias=fs[:, 320 + h:321 + h], scale=SCL),
                        inc=sac,
                        waits=[(spe, scm), (sdv, negb_mark)] + ptfree_m[sb])
                    sb_expmark[sb] = [(sac, exm)]
                    masked = (pos == 0) or (c >= 8)
                    if masked:
                        moff = c * 512 if pos == 0 else 4096 + (c - 8) * 512
                        mk = eop("ve", lambda e, sb=sb, moff=moff:
                                 e.tensor_mul(
                                     ptb[:, sb * 512:(sb + 1) * 512],
                                     ptb[:, sb * 512:(sb + 1) * 512],
                                     wb[:, moff:moff + 512]),
                                 inc=sdv, waits=[(sac, exm), (smk, MK)])
                        mask_end[0 if pos == 0 else 1] = [(sdv, mk)]
                        return [(sdv, mk)]
                    return [(sac, exm)]

                def emit_avsums(c, av_w):
                    sb = c % 3
                    first = (c == 0)
                    lastc = (c == n_c - 1)
                    sm_ = eop(
                        "te", lambda e, sb=sb, c=c, ph=ph, voff=voff,
                        first=first, lastc=lastc: (
                            e.matmul(p[2 + ph][:, :],
                                     kb[:, voff + c * 128:
                                        voff + (c + 1) * 128],
                                     ptb[:, sb * 512:(sb + 1) * 512],
                                     start=first, stop=lastc),
                            e.matmul(p[4 + ph][:, :], onem_ap,
                                     ptb[:, sb * 512:(sb + 1) * 512],
                                     start=first, stop=lastc))[-1],
                        inc=spe,
                        waits=av_w + ((avfree[ph] + sumfree[ph])
                                      if first else []))
                    ptfree_m[sb] = [(spe, sm_)]
                    return sm_

                av_w_q = []
                last_sums = None
                for c in range(n_c):
                    av_w_q.append(emit_score(c))
                    if c >= 2:
                        last_sums = emit_avsums(c - 2, av_w_q[c - 2])
                last_sums = emit_avsums(n_c - 2, av_w_q[n_c - 2])
                last_sums = emit_avsums(n_c - 1, av_w_q[n_c - 1])
                rbz = fs[:, 384:896] if ph == 0 else fs[:, 2048:2560]
                rcp = eop("ve", lambda e, ph=ph, z=rbz: e.reciprocal(
                    z, p[4 + ph][:, :]), inc=sdv,
                    waits=[(spe, last_sums)])
                oev = eop("ve", lambda e, ph=ph, h=h, pos=pos, z=rbz:
                          e.tensor_mul(
                              pA[:, h * 1024 + pos * 512:
                                 h * 1024 + pos * 512 + 512],
                              p[2 + ph][:, :], z),
                          inc=sdv,
                          waits=[(spe, last_sums), b2_end_pe, (sdv, rcp)])
                sumfree[ph] = [(sdv, rcp)]
                avfree[ph] = [(sdv, oev)]
                kvfree_c[ph] = [(spe, last_sums)]
        c_end_dv = (sdv, sdv.n)
        c_end_pe = (spe, spe.n)
        pmark[0] = sb_expmark[0]
        pmark[1] = sb_expmark[1]
        pmark[2] = avfree[0]
        pmark[3] = avfree[1]
        pmark[4] = sumfree[0]
        pmark[5] = sumfree[1]
        kbfree = [kvfree_c[0], kvfree_c[1]]
        afree = [list(ptfree_m[0]), list(ptfree_m[1])]
        s2_es.close()
        tp_alias_marks = (list(sb_expmark[2]) + list(tp_marks_ab))
        tp[0] = es.enter_context(nc.psum_tensor("tpd0", [128, 1024], BF16))
        tp[1] = es.enter_context(nc.psum_tensor("tpd1", [128, 1024], BF16))
        tpfree[0] = list(tp_alias_marks)
        tpfree[1] = list(tp_alias_marks)

        pend = [None]

        def flush_pend():
            if pend[0] is not None:
                pend[0]()
                pend[0] = None

        def tmproj_phase(mat, bias_base, rhs_dv_wait, pb_pe_wait,
                         slab_extra):
            for ob in range(H):
                pw = ob % 2
                wv = wslab(mat, ob * 128, wb, pw * 4096, sw[pw],
                           wfree[pw] + slab_extra[pw])
                gm = None
                for tq in range(2):
                    g = gctr[0] % 4
                    gctr[0] += 1
                    gm = mm_group(
                        g, 32,
                        lambda i, pw=pw: wb[:, pw * 4096 + i * 128:
                                            pw * 4096 + (i + 1) * 128],
                        lambda i, tq=tq: pA[:, i * 1024 + tq * 512:
                                            i * 1024 + tq * 512 + 512],
                        p[g][:, :],
                        [(sw[pw], wv), rhs_dv_wait])
                    flush_pend()
                    jp = (2 * ob + tq) % 2
                    af = ptb[:, jp * 512:(jp + 1) * 512]
                    ev = eop("sc", lambda e, g=g, af=af, ob=ob: e.activation(
                        af, p[g][:, :], AF.Identity,
                        bias=fs[:, bias_base + ob:bias_base + ob + 1]),
                        inc=sac, waits=[(spe, gm)] + afree[jp])
                    pmark[g] = [(sac, ev)]

                    def emit_tr(ev=ev, af=af, ob=ob, tq=tq, jp=jp):
                        T = tpctr[0] % 2
                        tpctr[0] += 1
                        bank = tp[T]

                        def mk(e, bank=bank, af=af):
                            for j in range(4):
                                i = e.transpose(
                                    bank[:, j * 128:(j + 1) * 128],
                                    af[:, j * 128:(j + 1) * 128], idn_ap)
                            return i
                        tm = eop("te", mk, inc=spe,
                                 waits=[(sac, ev)] + tpfree[T])
                        afree[jp] = [(spe, tm)]
                        evv = eop("ve",
                                  lambda e, bank=bank, ob=ob, tq=tq:
                                  e.tensor_copy(
                                      pB3[:, tq * 4:(tq + 1) * 4,
                                          ob * 128:(ob + 1) * 128],
                                      bank[:, 0:512].rearrange(
                                          "q (j c) -> q j c", j=4)),
                                  inc=sdv,
                                  waits=[(spe, tm), pb_pe_wait])
                        tpfree[T] = [(sdv, evv)]
                    pend[0] = emit_tr
                wfree[pw] = [(spe, gm)]
            flush_pend()

        tmproj_phase(woT, BO, c_end_dv, c_end_pe,
                     [mask_end[0], mask_end[1]])
        d1_end_dv = (sdv, sdv.n)
        d1_end_pe = (spe, spe.n)

        for tt in range(NT):
            pa = tt % 2
            xt = xb[:, pa * 4096:(pa + 1) * 4096]
            lv = edma("sy", sx[pa],
                      lambda e, t=tt, d=xt: e.dma_start(
                          out=d, in_=xo.ap()[t * 128:(t + 1) * 128, :]),
                      waits=xfree[pa] + [d1_end_dv])
            cs0 = 224 + pa * 16
            sf0 = [statfree[pa]] if statfree[pa] else []
            ad = eop("ve", lambda e, t=tt, x=xt, c=cs0:
                     e.scalar_tensor_tensor(
                         x, pB[:, t * 4096:(t + 1) * 4096], 1.0, x,
                         ALU.mult, ALU.add,
                         accum_out=fs[:, c:c + 1]),
                inc=sdv, waits=[(sx[pa], lv), d1_end_dv] + sf0)
            sv = edma("sy", sx1w,
                      lambda e, t=tt, x=xt: e.dma_start(
                          out=x1d.ap()[t], in_=x),
                      waits=[(sdv, ad)])
            apm = ln_tile(xt, pa, lnt_of[pa], [(sdv, ad)], kbfree[pa],
                          sx_pre=(sdv, ad))
            xfree[pa] = [(sac, apm), (sx1w, sv)]
            transpose_out(
                lambda j, L=lnt_of[pa]: L[:, j * 128:(j + 1) * 128], 32,
                lambda g0, gw, t=tt: pB[:, t * 4096 + g0 * 128:
                                        t * 4096 + (g0 + gw) * 128]
                .rearrange("q (j c) -> q j c", j=gw),
                [(sac, apm), (sdv, ad)])
            kbfree[pa] = [(spe, spe.n)]
        d1b_dv = (sdv, sdv.n)

        d2free = [wfree[0] + wfree[1] + mask_end[1],
                  kbfree[0] + kbfree[1]]
        ssfree = [[], []]
        d2lastg = [None, None]
        for mc in range(H):
            bsel = mc % 2
            buf = wb if bsel == 0 else kb
            ssem = sw[0] if bsel == 0 else skv[0]
            wl = d2free[bsel] if d2lastg[bsel] is None else \
                [(spe, d2lastg[bsel])]
            w1 = wslab(w12T, mc * 128, buf, 0, ssem, wl)
            w2 = wslab(w12T, (mc + 32) * 128, buf, 4096, ssem, wl)
            for tq in range(2):
                pm = (2 * mc + tq) % 2
                ssz = fs[:, 1408:1920] if pm == 0 else fs[:, 2560:3072]
                gA = gctr[0] % 4
                gctr[0] += 1
                gB = gctr[0] % 4
                gctr[0] += 1
                gmA = mm_group(
                    gA, 32,
                    lambda i, buf=buf: buf[:, i * 128:(i + 1) * 128],
                    lambda i, tq=tq: pB3[:, tq * 4:(tq + 1) * 4,
                                         i * 128:(i + 1) * 128],
                    p[gA][:, :],
                    [(ssem, w2), d1b_dv])
                gmB = mm_group(
                    gB, 32,
                    lambda i, buf=buf: buf[:, 4096 + i * 128:
                                           4096 + (i + 1) * 128],
                    lambda i, tq=tq: pB3[:, tq * 4:(tq + 1) * 4,
                                         i * 128:(i + 1) * 128],
                    p[gB][:, :],
                    [(ssem, w2), d1b_dv])
                sil = eop("sc", lambda e, gA=gA, mc=mc, z=ssz: e.activation(
                    z, p[gA][:, :], AF.Silu,
                    bias=fs[:, B12 + mc:B12 + mc + 1]),
                    inc=sac, waits=[(spe, gmA)] + ssfree[pm])
                x4m = eop("ve", lambda e, gB=gB, mc=mc, tq=tq, z=ssz:
                          e.scalar_tensor_tensor(
                              pA[:, mc * 1024 + tq * 512:
                                 mc * 1024 + tq * 512 + 512],
                              p[gB][:, :],
                              fs[:, B12 + 32 + mc:B12 + 32 + mc + 1],
                              z, ALU.add, ALU.mult),
                          inc=sdv,
                          waits=[(spe, gmB), (sac, sil), d1_end_pe])
                pmark[gA] = [(sac, sil)]
                pmark[gB] = [(sdv, x4m)]
                ssfree[pm] = [(sdv, x4m)]
                d2lastg[bsel] = gmB
        d2_end_dv = (sdv, sdv.n)
        d2_end_pe = (spe, spe.n)

        wfree[0] = [(spe, d2lastg[0])]
        wfree[1] = [(spe, d2lastg[0])]
        tmproj_phase(w3T, B3, d2_end_dv, d2_end_pe, [[], []])
        d3_end_dv = (sdv, sdv.n)

        for tt in range(NT):
            pa = tt % 2
            xt = xb[:, pa * 4096:(pa + 1) * 4096]
            lv = edma("sy", sx[pa],
                      lambda e, t=tt, d=xt: e.dma_start(
                          out=d, in_=x1d.ap()[t]),
                      waits=xfree[pa] + [(sx1w, sx1w.n), d3_end_dv])
            ad = eop("ve", lambda e, t=tt, x=xt: e.tensor_add(
                x, pB[:, t * 4096:(t + 1) * 4096], x),
                inc=sdv, waits=[(sx[pa], lv), d3_end_dv])
            edma("sy", sst,
                 lambda e, t=tt, x=xt: e.dma_start(
                     out=out.ap()[t * 128:(t + 1) * 128, :], in_=x),
                 waits=[(sdv, ad)])
            xfree[pa] = [(sst, sst.n)]
        ew("sy", sst, sst.n)

        with nc.Block() as block:
            @block.sync
            def _(e):
                for en, f in sched:
                    if en == "sy":
                        f(e)

            @block.gpsimd
            def _(e):
                for en, f in sched:
                    if en == "gp":
                        f(e)

            @block.tensor
            def _(e):
                for en, f in sched:
                    if en == "te":
                        f(e)

            @block.scalar
            def _(e):
                for en, f in sched:
                    if en == "sc":
                        f(e)

            @block.vector
            def _(e):
                for en, f in sched:
                    if en == "ve":
                        f(e)
    finally:
        es.close()
    return nc


def _host_masks(par):
    qg = (0, 3) if par == 0 else (1, 2)
    m = np.zeros((16, 128, 512), np.float32)
    q = np.arange(512)[None, :]
    kk = np.arange(128)[:, None]
    for c in range(8):
        m[c] = ((c * 128 + kk) <= (qg[0] * 512 + q))
    for j in range(8):
        c = 8 + j
        m[c] = ((c * 128 + kk) <= (qg[1] * 512 + q))
    return np.ascontiguousarray(m.astype(BF))


def kernel(L0_attn_x, L0_attn_Wq, L0_attn_bq, L0_attn_Wk, L0_attn_bk,
           L0_attn_Wv, L0_attn_bv, L0_attn_Wo, L0_attn_bo,
           L0_ffn_w12, L0_ffn_b12, L0_ffn_w3, L0_ffn_b3):
    x = np.asarray(L0_attn_x, np.float32)

    def T(w):
        wt = np.asarray(w, np.float32).T
        nb = wt.shape[1] // 128
        return np.ascontiguousarray(
            wt.reshape(32, 128, nb, 128).transpose(2, 1, 0, 3))

    def shape_bias(b, n):
        return np.ascontiguousarray(
            np.asarray(b, np.float32).reshape(n, 128).T)

    wq, wk, wv, wo = (T(w) for w in
                      (L0_attn_Wq, L0_attn_Wk, L0_attn_Wv, L0_attn_Wo))
    w12, w3 = T(L0_ffn_w12), T(L0_ffn_w3)
    bq, bk, bv, bo, b3 = (shape_bias(b, 32) for b in
                          (L0_attn_bq, L0_attn_bk, L0_attn_bv, L0_attn_bo,
                           L0_ffn_b3))
    b12 = shape_bias(L0_ffn_b12, 64)
    idn = np.eye(128, dtype=BF)
    onec = np.ones((128, 1), dtype=BF)
    onem = np.ones((128, 128), dtype=BF)
    onecf = np.ones((128, 1), dtype=np.float32)
    oner = np.ones((1, 128), dtype=np.float32)
    msk = [_host_masks(0), _host_masks(1)]

    nc = _build()
    in_maps = []
    for c in range(8):
        seq, par = c // 2, c % 2
        xseq = np.ascontiguousarray(x[seq])
        if par == 0:
            xown = np.ascontiguousarray(
                np.concatenate([x[seq, 0:512], x[seq, 1536:2048]], axis=0))
        else:
            xown = np.ascontiguousarray(x[seq, 512:1536])
        in_maps.append(dict(
            x_own=xown, x_seq=xseq, wqT=wq, wkT=wk, wvT=wv, woT=wo,
            w12T=w12, w3T=w3, bqs=bq, bks=bk, bvs=bv, bos=bo, b3s=b3,
            b12s=b12, masks=msk[par], idn=idn, onec=onec, onecf=onecf,
            oner=oner, onem=onem))
    res = run_bass_kernel_spmd(nc, in_maps, core_ids=list(range(8)))
    y = np.empty((4, 2048, 4096), np.float32)
    for c in range(8):
        seq, par = c // 2, c % 2
        o = np.asarray(res.results[c]["out"])
        if par == 0:
            y[seq, 0:512] = o[0:512]
            y[seq, 1536:2048] = o[512:1024]
        else:
            y[seq, 512:1536] = o
    return y
